# version 05 (final)
# baseline (speedup 1.0000x reference)
import sys as _sys
import numpy as np
from contextlib import ExitStack

import concourse.bass as bass
import concourse.mybir as mybir
from concourse.bass import IndirectOffsetOnAxis
from concourse.bass_utils import run_bass_kernel_spmd

F32 = mybir.dt.float32
BF16 = mybir.dt.bfloat16
I32 = mybir.dt.int32
AF = mybir.ActivationFunctionType
ALU = mybir.AluOpType
AX = mybir.AxisListType

NCORES = 8
B = 16
TG = 2048
NPC = TG // NCORES
D = 1024
TPC = B * NPC
NG = TPC // 128

IH, IHD = 4, 256
E, CAP, FF = 8, 4096, 4096
PH, PHD = 16, 64
CONVK, NST = 4, 64
HALO = 64
HID, BK = 4096, 32
LW = HALO + NPC
SCW = LW - 16
TINY = float(2.0 ** -126)

MAGIC = float(np.float32(1.5 * 2 ** 23))
LN2 = float(np.log(2.0))
INV_LN2 = float(1.0 / np.log(2.0))

STOP_AFTER = None
DBG_OUT = {}
DBG_HOOK = [None]


class Op:
    __slots__ = ("eng", "fn", "waits", "incs", "dma", "lineno")

    def __init__(self, eng, fn):
        self.eng, self.fn, self.waits, self.incs = eng, fn, [], []


class Eng:

    AUTO = ("vector", "scalar", "gpsimd")

    NPOOL = 6

    def __init__(self, prog, name):
        self.p, self.name = prog, name
        self.sem = prog.nc.semaphore(f"sem_{name}").__enter__()
        self.dpool = [prog.nc.semaphore(f"dsem_{name}{i}").__enter__()
                      for i in range(Eng.NPOOL)]
        self.dcum = [0] * Eng.NPOOL
        self.dslot = 0
        self.in_group = False
        self.cnt = 0
        self.last = None
        self.pending = []
        self.auto = name in Eng.AUTO

    def wait(self, *toks):
        for t in toks:
            if t is None:
                continue
            if t not in self.pending:
                self.pending.append(t)

    def op(self, method, *args, _dma=False, **kw):
        if self.auto and not _dma and self.cnt > 0:
            self.wait((self.sem, self.cnt))
        o = Op(self.name, (method, args, kw))
        o.lineno = _sys._getframe(1).f_lineno if method != "dma_start" else _sys._getframe(2).f_lineno
        o.waits = self.pending
        self.pending = []
        o.dma = _dma
        self.p.ops.append(o)
        self.last = o
        if self.auto and not _dma:
            self.cnt += 1
            o.incs.append((self.sem, 1))
        return o

    def dma_op(self, method, *args, **kw):
        if not self.in_group:
            self.dslot = (self.dslot + 1) % Eng.NPOOL
            if self.dcum[self.dslot] > 0:
                self.wait((self.dpool[self.dslot], self.dcum[self.dslot]))
            self.in_group = True
        o = self.op(method, *args, _dma=True, **kw)
        self.dcum[self.dslot] += 16
        o.incs.append((self.dpool[self.dslot], 16))
        return o

    def dma(self, out, in_, **kw):
        if not self.in_group:
            self.dslot = (self.dslot + 1) % Eng.NPOOL
            if self.dcum[self.dslot] > 0:
                self.wait((self.dpool[self.dslot], self.dcum[self.dslot]))
            self.in_group = True
        o = self.op("dma_start", _dma=True, out=out, in_=in_, **kw)
        self.dcum[self.dslot] += 16
        o.incs.append((self.dpool[self.dslot], 16))
        return o

    def dtok(self):
        self.in_group = False
        return (self.dpool[self.dslot], self.dcum[self.dslot])

    def tok(self):
        o = self.last
        assert o is not None
        if o.dma:
            return self.dtok()
        if self.auto:
            return (self.sem, self.cnt)
        self.cnt += 1
        o.incs.append((self.sem, 1))
        return (self.sem, self.cnt)


class Prog:
    def __init__(self, nc):
        self.nc = nc
        self.ops = []
        self.T = Eng(self, "tensor")
        self.V = Eng(self, "vector")
        self.S = Eng(self, "scalar")
        self.G = Eng(self, "gpsimd")
        self.Y = Eng(self, "sync")

    def play(self):
        by = {}
        for o in self.ops:
            by.setdefault(o.eng, []).append(o)
        with self.nc.Block() as block:
            for name, ops in by.items():
                def mk(ops):
                    def section(eng):
                        for o in ops:
                            for (sem, val) in o.waits:
                                eng.wait_ge(sem, val)
                            method, args, kw = o.fn
                            try:
                                inst = getattr(eng, method)(*args, **kw)
                            except Exception as e:
                                raise RuntimeError(
                                    f"op {method} recorded at kernel.py:{o.lineno}: {e}")
                            for (sem, amt) in o.incs:
                                inst.then_inc(sem, amt)
                    return section
                getattr(block, name)(mk(ops))
        self.ops = []


def AP3(t, offset, pattern):
    if isinstance(t, bass.AP):
        return bass.AP(t.tensor, t.offset + offset, pattern)
    return bass.AP(t, offset, pattern)


def all_dtoks(E):
    E.in_group = False
    return [(E.dpool[i], E.dcum[i]) for i in range(Eng.NPOOL) if E.dcum[i] > 0]


def build_program():
    nc = bass.Bass("TRN2", target_bir_lowering=False)
    p = Prog(nc)
    T, V, S, G, Y = p.T, p.V, p.S, p.G, p.Y
    es = ExitStack()
    dram = {}

    def din(name, shape, dtype=F32):
        dram[name] = nc.dram_tensor(name, shape, dtype, kind="ExternalInput").ap()
        return dram[name]

    def dout(name, shape, dtype=F32):
        dram[name] = nc.dram_tensor(name, shape, dtype, kind="ExternalOutput").ap()
        return dram[name]

    def dtmp(name, shape, dtype=F32):
        dram[name] = nc.dram_tensor(name, shape, dtype).ap()
        return dram[name]

    def sb(name, shape, dtype):
        return es.enter_context(nc.sbuf_tensor(name, shape, dtype))

    def psum(name, shape, dtype=F32):
        return es.enter_context(nc.psum_tensor(name, shape, dtype))

    x0tok = din("x0tok", [TPC, D])
    x0T = din("x0T", [D, TPC])
    w_qkvT = din("w_qkvT", [D, 3 * D])
    b_qkv = din("b_qkv", [1, 3 * D])
    w_oT = din("w_oT", [D, D])
    b_o = din("b_o", [1, D])
    ln1_g = din("ln1_g", [1, D]); ln1_b = din("ln1_b", [1, D])
    ln2_g = din("ln2_g", [1, D]); ln2_b = din("ln2_b", [1, D])
    wg_d = din("wg", [D, E])
    w_q = din("w_q", [D, D]); b_q = din("b_q", [1, D])
    w_k = din("w_k", [D, D]); b_k = din("b_k", [1, D])
    w_v = din("w_v", [D, D]); b_v = din("b_v", [1, D])
    w_ao = din("w_ao", [D, D]); b_ao = din("b_ao", [1, D])
    w1e = din("w1e", [D, FF]); b1e = din("b1e", [1, FF])
    w2e = din("w2e", [FF, D]); b2e = din("b2e", [1, D])
    maskP = din("maskP", [128, 512])
    mham_mask = din("mham_mask", [128, 512])
    conv_w = din("conv_w", [CONVK, D])
    conv_b = din("conv_b", [1, D])
    wp_d = din("wp", [D, 2 * NST]); bp_d = din("bp", [1, 2 * NST])
    ws_d = din("ws", [D, NST]); bs_d = din("bs", [1, NST])
    a_log = din("a_log", [1, NST]); dt_log = din("dt_log", [1, NST])
    w_loT = din("w_loT", [NST, D]); b_lo = din("b_lo", [1, D])
    w_fc = din("w_fc", [HID, D]); b_fc = din("b_fc", [1, HID])
    w_prT = din("w_prT", [HID, D])
    b_pr = din("b_pr", [1, D])
    w_pr = din("w_pr", [D, HID])
    halo_idx = din("halo_idx", [B * HALO, 1], I32)
    scan_mask = din("scan_mask", [1, SCW])
    out_d = dout("out", [TPC, D])
    for name, (shape, dt_) in DBG_OUT.items():
        dout(name, shape, dt_)

    stat_rows = dtmp("stat_rows", [4, TPC])
    y1tok_f = dtmp("y1tok_f", [TPC, D])
    y1tok_b = dtmp("y1tok_b", [TPC, D], BF16)
    s_tok = dtmp("s_tok", [TPC, D])
    g_tok = dtmp("g_tok", [TPC, D])
    x1tok_d = dtmp("x1tok_d", [TPC, D])
    rs_in = dtmp("rs_in", [NCORES * CAP + 128, D], BF16)
    rs_out = dtmp("rs_out", [CAP, D], BF16)
    cnt_in = dtmp("cnt_in", [B, E])
    cnt_out = dtmp("cnt_out", [NCORES * B, E])
    yexpT_d = dtmp("yexpT_d", [D, CAP], BF16)
    yret_in = dtmp("yret_in", [CAP, D], BF16)
    yret_out = dtmp("yret_out", [NCORES * CAP, D], BF16)
    xn1tok_b = dtmp("xn1tok_b", [TPC, D], BF16)
    TAILR = B * HALO + 16
    tail_in = dtmp("tail_in", [TAILR, D], BF16)
    tail_out = dtmp("tail_out", [NCORES * TAILR, D], BF16)
    wfcq_d = dtmp("wfcq_d", [HID, D], BF16)
    wprq_d = dtmp("wprq_d", [D, HID], BF16)
    xq_d = dtmp("xq_d", [TPC, D], BF16)
    hq_d = dtmp("hq_d", [TPC, HID], BF16)
    s1tok_d = dtmp("s1tok_d", [TPC, D])
    csg_row = dtmp("csg_row", [2, 8])
    zero_d = dtmp("zero_d", [1, D], BF16)
    halo_d = dtmp("halo_d", [B * HALO, D], BF16)
    ws_b = dtmp("ws_b", [D, NST], BF16)
    csc_d = dtmp("csc_d", [64, SCW])
    w1b_d = dtmp("w1b_d", [D, FF], BF16)
    w2b_d = dtmp("w2b_d", [FF, D], BF16)
    off_row = dtmp("off_row", [B, E])

    xnT_d = dtmp("xnT_d", [D, TPC], BF16)

    zarr = sb("zarr", [128, NG * 16], F32)
    slot_i = sb("slot_i", [128, NG], I32)
    slotg_i = sb("slotg_i", [128, NG], I32)
    iseq_sb = sb("iseq_sb", [128, NG * 8], F32)
    wA = sb("wA", [128, 8 * 3072], BF16)
    wB = sb("wB", [128, 8 * 1024], BF16)
    bias_b = sb("bias_b", [1, 4096], BF16)
    onesb = sb("onesb", [1, 512], BF16)
    ones32 = sb("ones32", [128, 128], BF16)
    zerosb = sb("zerosb", [128, 128], BF16)
    ident = sb("ident", [128, 128], BF16)
    identf = sb("identf", [128, 128], F32)
    misc = sb("misc", [128, 64], F32)
    gvec = sb("gvec", [128, 80], F32)
    lnv = sb("lnv", [128, 32], F32)
    mham = sb("mham", [128, 512], BF16)
    sqrow = sb("sqrow", [1, 512], F32)
    g2row = sb("g2row", [128, 2 * D], F32)
    zr_row = sb("zr_row", [1, D], BF16)
    maskP_b = sb("maskP_b", [128, 512], BF16)
    strow = sb("strow", [16, 512], F32)

    PS = [psum(f"ps{i}", [128, 512]) for i in range(6)]
    PT = psum("pst", [128, 128])
    PTB = psum("pstb", [128, 128], BF16)
    bg = {}

    def guard(eng, key):
        t = bg.get(key)
        if t is not None:
            eng.wait(t)

    def release(key, t):
        bg[key] = t

    G.op("iota", misc[:, 0:1], [[1, 1]], channel_multiplier=1,
         allow_small_or_imprecise_dtypes=True)
    G.op("iota", misc[:, 8:16], [[1, 8]], channel_multiplier=0,
         allow_small_or_imprecise_dtypes=True)
    G.op("iota", identf[:, :], [[1, 128]], channel_multiplier=0,
         allow_small_or_imprecise_dtypes=True)
    tg = G.tok()
    V.wait(tg)
    V.op("memset", misc[:, 1:2], 1.0)
    V.op("tensor_scalar", identf[:, :], identf[:, :], misc[:, 0:1], None, ALU.is_equal)
    V.op("tensor_copy", ident[:, :], identf[:, :])
    V.op("memset", ones32[:, :], 1.0)
    V.op("memset", zerosb[:, :], 0.0)
    V.op("memset", onesb[0:1, :], 1.0)
    V.op("memset", misc[:, 34:35], -16.0)
    t_ident = V.tok()

    for j in range(8):
        G.dma(out=wA[:, j * 3072:(j + 1) * 3072], in_=w_qkvT[j * 128:(j + 1) * 128, :])
        G.dma(out=wB[:, j * 1024:(j + 1) * 1024], in_=w_oT[j * 128:(j + 1) * 128, :])
    G.dma(out=bias_b[0:1, 0:3072], in_=b_qkv)
    G.dma(out=bias_b[0:1, 3072:4096], in_=b_o)
    G.dma(out=mham[:, :], in_=mham_mask)
    G.dma(out=maskP_b[:, :], in_=maskP)
    t_wffn = G.tok()

    wg_sb = sb("wg_sb", [128, 8 * E], F32)
    for j in range(8):
        Y.dma(out=wg_sb[:, j * E:(j + 1) * E], in_=wg_d[j * 128:(j + 1) * 128, :])
    for k, src in enumerate([ln1_g, ln1_b, ln2_g, ln2_b]):
        for j in range(8):
            Y.dma(out=lnv[:, k * 8 + j:k * 8 + j + 1],
                  in_=AP3(src, j * 128, [[1, 128], [1, 1]]))
    t_lnv = Y.tok()
    S.wait(t_lnv)
    S.dma(out=g2row[:, 0:D], in_=ln2_g.broadcast_to([128, D]))
    S.dma(out=g2row[:, D:2 * D], in_=ln2_b.broadcast_to([128, D]))
    t_g2 = S.dtok()
    V.wait(t_lnv, t_ident)
    V.op("memset", zr_row[0:1, :], 0.0)
    V.op("memset", misc[:, 35:36], -12.0)
    V.op("memset", misc[:, 37:38], 3.0 * LN2)
    V.op("memset", misc[:, 38:39], -3.0 * LN2)
    V.op("tensor_scalar", misc[:, 36:37], misc[:, 0:1], 32768.0, None, ALU.add)
    for j in range(8):
        V.op("tensor_scalar", gvec[:, 10 * j:10 * j + 8], wg_sb[:, j * E:(j + 1) * E],
             lnv[:, j:j + 1], None, ALU.mult)
        V.op("memset", gvec[:, 10 * j + 8:10 * j + 9], 1.0)
        V.op("memset", gvec[:, 10 * j + 9:10 * j + 10], 0.0)
    t_gvec = V.tok()
    T.wait(t_gvec)
    for j in range(8):
        T.op("matmul", PS[0][0:8, 0:1], gvec[:, 10 * j:10 * j + 8], misc[:, 1:2],
             start=(j == 0), stop=(j == 7), skip_group_check=True)
    for j in range(8):
        T.op("matmul", PS[0][0:8, 1:2], wg_sb[:, j * E:(j + 1) * E], lnv[:, 8 + j:9 + j],
             start=False, stop=(j == 7), skip_group_check=True)
    t_csg = T.tok()
    V.wait(t_csg)
    V.op("tensor_copy", misc[0:8, 32:34], PS[0][0:8, 0:2])
    t_csgc = V.tok()
    release(0, t_csgc)
    Y.wait(t_csgc)
    Y.dma(out=AP3(csg_row, 0, [[1, 8], [8, 2]]), in_=AP3(misc, 32, [[64, 8], [1, 2]]),
          allow_slow_non_contiguous=True)
    t_csgr = Y.tok()
    S.wait(t_csgr)
    S.dma(out=misc[:, 16:24], in_=csg_row[0:1, 0:8].broadcast_to([128, 8]))
    S.dma(out=misc[:, 24:32], in_=csg_row[1:2, 0:8].broadcast_to([128, 8]))
    t_csgb = S.tok()

    TT_ = 512
    NTi = TPC // TT_
    es1 = ExitStack()
    es1a = ExitStack()

    def sb1(name, shape, dtype):
        return es1.enter_context(nc.sbuf_tensor(name, shape, dtype))

    def sb1a(name, shape, dtype):
        return es1a.enter_context(nc.sbuf_tensor(name, shape, dtype))

    st4 = sb1("st4", [128, 16], F32)
    sct = sb1("sct", [128, 16], F32)
    xnTf = sb1("xnTf", [128, 8 * TT_], F32)
    xnTb = sb1("xnTb", [128, 8 * TT_], BF16)
    qT_t = sb1("qT_t", [128, 8 * TT_], BF16)
    kT_t = sb1("kT_t", [128, 8 * TT_], BF16)
    vtok = sb1("vtok", [128, 4 * D], BF16)
    expst = sb1("expst", [128, 4 * 512], BF16)
    bc_b = sb1("bc_b", [128, TT_], F32)
    y1gb = sb1("y1gb", [128, D], BF16)
    cntb = sb1("cntb", [128, 8], BF16)
    x0t = sb1a("x0t", [128, 4 * D], F32)
    sqbuf = sb1a("sqbuf", [128, D], F32)
    preT = sb1a("preT", [128, 8 * TT_], F32)
    y1Tt = preT
    oT_t = sb1a("oT_t", [128, 8 * TT_], BF16)
    recip1 = sb1a("recip1", [128, 512], F32)
    sq2 = sb1a("sq2", [128, 2 * TT_], F32)
    bc_a = sb1a("bc_a", [128, TT_], F32)
    ztk = sb1a("ztk", [128, 64], F32)
    y1g = sb1a("y1g", [128, D], F32)

    def tokrow_ap(dr, ti, sub, width):
        base = (4 * sub * NPC + ti * 32) * width
        return AP3(dr, base, [[NPC * width, 4], [width, 32], [1, width]])


    def p1_tile(ti):
        V.op("memset", st4[:, 4:8], 0.0)
        t_stz = V.tok()
        S.wait(t_stz)
        if bg.get("x0t") is not None:
            Y.wait(bg["x0t"])
        for sub in range(4):
            Y.dma(out=x0t[:, sub * D:(sub + 1) * D], in_=tokrow_ap(x0tok, ti, sub, D))
        tld = Y.dtok()
        V.wait(tld)
        S.wait(tld)
        for sub in range(4):
            V.op("reduce_sum", st4[:, sub:sub + 1], x0t[:, sub * D:(sub + 1) * D], axis=AX.X)
            S.op("activation", sqbuf[:, :], x0t[:, sub * D:(sub + 1) * D],
                 AF.Square, accum_out=st4[:, 4 + sub:5 + sub])
        bg["x0t"] = S.tok()
        tS = bg["x0t"]
        V.wait(tS)
        V.op("tensor_scalar", st4[:, 8:12], st4[:, 0:4], -1.0 / D, None, ALU.mult)
        V.op("tensor_tensor", sct[:, 0:4], st4[:, 8:12], st4[:, 8:12], ALU.mult)
        V.op("tensor_scalar", sct[:, 4:8], st4[:, 4:8], 1.0 / D, None, ALU.mult)
        V.op("tensor_tensor", sct[:, 0:4], sct[:, 4:8], sct[:, 0:4], ALU.subtract)
        V.op("tensor_scalar", sct[:, 0:4], sct[:, 0:4], 1e-5, None, ALU.add)
        V.op("reciprocal", sct[:, 8:12], sct[:, 0:4])
        tV = V.tok()
        S.wait(tV)
        S.op("activation", st4[:, 12:16], sct[:, 8:12], AF.Sqrt)
        tS = S.tok()
        V.wait(tS)
        V.op("tensor_tensor", st4[:, 8:12], st4[:, 8:12], st4[:, 12:16], ALU.mult)
        tV = V.tok()
        T.wait(tV, t_ident)
        for (cb, drow) in ((8, 0), (12, 1)):
            guard(T, "PT")
            guard(T, "strow")
            T.op("transpose", PT[0:4, 0:128], st4[:, cb:cb + 4], identf[:, :])
            tT = T.tok()
            V.wait(tT)
            V.op("tensor_copy", strow[0:4, 0:128], PT[0:4, 0:128])
            tV2 = V.tok()
            release("PT", tV2)
            Y.wait(tV2)
            Y.dma(out=AP3(stat_rows, drow * TPC + ti * TT_, [[128, 4], [1, 128]]),
                  in_=strow[0:4, 0:128])
            release("strow", Y.dtok())
        t_str = Y.dtok()
        S.wait(t_str)
        S.dma(out=bc_a[:, :], in_=stat_rows[1:2, ti * TT_:(ti + 1) * TT_].broadcast_to([128, TT_]))
        S.dma(out=bc_b[:, :], in_=stat_rows[0:1, ti * TT_:(ti + 1) * TT_].broadcast_to([128, TT_]))
        t_bc = S.tok()
        if STOP_AFTER == "p1_stats":
            return
        Y.wait(bg.get("xnTf"))
        for j in range(8):
            Y.dma(out=xnTf[:, j * TT_:(j + 1) * TT_],
                  in_=AP3(x0T, (j * 128) * TPC + ti * 32, [[TPC, 128], [NPC, B], [1, 32]]))
        t_x0T = Y.tok()
        V.wait(t_x0T, t_bc)
        for j in range(8):
            V.op("tensor_tensor", xnTf[:, j * TT_:(j + 1) * TT_],
                 xnTf[:, j * TT_:(j + 1) * TT_], bc_a[:, :], ALU.mult)
            V.op("tensor_tensor", xnTf[:, j * TT_:(j + 1) * TT_],
                 xnTf[:, j * TT_:(j + 1) * TT_], bc_b[:, :], ALU.add)
            V.op("tensor_copy", xnTb[:, j * TT_:(j + 1) * TT_],
                 AP3(xnTf, j * TT_, [[8 * TT_, 128], [2, 16], [1, 2], [32, 16]]))
        t_xn = V.tok()
        G.wait(t_xn)
        for j in range(8):
            G.dma(out=AP3(xnT_d, j * 128 * TPC + ti * TT_, [[TPC, 128], [1, TT_]]),
                  in_=xnTf[:, j * TT_:(j + 1) * TT_])
        bg["xnTf"] = G.dtok()
        if STOP_AFTER == "p1_xn":
            return
        T.wait(t_xn, t_wffn)
        for fc in range(16):
            bank = PS[fc % 2]
            guard(T, fc % 2)
            for j in range(8):
                T.op("matmul", bank[:, 0:TT_],
                     wA[:, j * 3072 + fc * 128:j * 3072 + (fc + 1) * 128],
                     xnTb[:, j * TT_:(j + 1) * TT_],
                     start=(j == 0), stop=False, skip_group_check=True)
            T.op("matmul", bank[:, 0:TT_],
                 bias_b[0:1, fc * 128:(fc + 1) * 128], onesb[0:1, 0:TT_],
                 start=False, stop=True, skip_group_check=True)
            tmm = T.tok()
            dst = qT_t if fc < 8 else kT_t
            V.wait(tmm)
            V.op("tensor_copy", dst[:, (fc % 8) * TT_:((fc % 8) + 1) * TT_], bank[:, 0:TT_])
            release(fc % 2, V.tok())
        t_qk = (V.sem, V.cnt)
        for sub in range(4):
            for half in range(2):
                bk = 2 + ((2 * sub + half) % 2)
                bank = PS[bk]
                guard(T, bk)
                for j in range(8):
                    T.op("matmul", bank[:, 0:512],
                         xnTb[:, j * TT_ + sub * 128:j * TT_ + (sub + 1) * 128],
                         wA[:, j * 3072 + 2048 + half * 512:j * 3072 + 2048 + (half + 1) * 512],
                         start=(j == 0), stop=False, skip_group_check=True)
                T.op("matmul", bank[:, 0:512], onesb[0:1, 0:128],
                     bias_b[0:1, 2048 + half * 512:2048 + (half + 1) * 512],
                     start=False, stop=True, skip_group_check=True)
                tmm = T.tok()
                V.wait(tmm)
                V.op("tensor_copy",
                     vtok[:, sub * D + half * 512:sub * D + (half + 1) * 512],
                     bank[:, 0:512])
                release(bk, V.tok())
        t_v = (V.sem, V.cnt)
        if STOP_AFTER == "p1_qkv":
            return
        T.wait(t_qk)
        for h in range(IH):
            guard(T, 2 + h)
            T.op("matmul", PS[2 + h][:, :], zerosb[:, :], kT_t[:, 0:TT_],
                 start=True, stop=False, skip_group_check=True)
        for a in range(16):
            q4 = a % 4
            for h in range(IH):
                sc = PS[2 + h]
                for par in range(2):
                    for half in range(2):
                        ch = 2 * h + half
                        _c = a * 32 + par * 16
                        T.op("matmul",
                             sc[32 * q4:32 * q4 + 32, _c:_c + 16],
                             kT_t[:, ch * TT_ + a * 32:ch * TT_ + (a + 1) * 32],
                             qT_t[:, ch * TT_ + a * 32 + par * 16:ch * TT_ + a * 32 + par * 16 + 16],
                             start=False,
                             stop=(a == 15 and par == 1 and half == 1),
                             tile_position=(0, 32 * q4), skip_group_check=True)
        t_sc = T.tok()
        if STOP_AFTER == "p1_sc":
            return
        S.wait(t_sc)
        for h in range(IH):
            S.op("activation", expst[:, h * 512:(h + 1) * 512], PS[2 + h][:, :],
                 AF.Exp, bias=misc[:, 34:35], scale=1.0 / 16.0)
        t_exp = S.tok()
        V.wait(t_exp, t_wffn)
        for h in range(IH):
            V.op("tensor_tensor", expst[:, h * 512:(h + 1) * 512],
                 expst[:, h * 512:(h + 1) * 512], mham[:, :], ALU.mult)
        t_msk = V.tok()
        for h in range(IH):
            release(2 + h, t_msk)
        if STOP_AFTER == "p1_msk":
            return
        for h in range(IH):
            den = PS[5]
            T.wait(t_msk)
            guard(T, 5)
            for a in range(16):
                T.op("matmul", den[:, a * 32:(a + 1) * 32],
                     ones32[:, :],
                     expst[:, h * 512 + a * 32:h * 512 + (a + 1) * 32],
                     start=(a == 0), stop=(a == 15), skip_group_check=True)
            t_den = T.tok()
            V.wait(t_den, bg.get("recip_used"))
            V.op("reciprocal", recip1[:, :], den[:, :])
            t_rc = V.tok()
            release(5, t_rc)
            if STOP_AFTER == "p1_den":
                return
            for half in range(2):
                ch = 2 * h + half
                av = PS[ch % 2]
                guard(T, ch % 2)
                for a in range(16):
                    _cv = (a // 4) * D + h * 256 + half * 128
                    T.op("matmul", av[:, a * 32:(a + 1) * 32],
                         vtok[:, _cv:_cv + 128],
                         expst[:, h * 512 + a * 32:h * 512 + (a + 1) * 32],
                         start=(a == 0), stop=(a == 15), skip_group_check=True)
                tmm = T.tok()
                V.wait(tmm)
                V.op("scalar_tensor_tensor", oT_t[:, ch * TT_:(ch + 1) * TT_],
                     av[:, :], 1.0, recip1[:, :], ALU.mult, ALU.mult)
                tev = V.tok()
                release(ch % 2, tev)
                bg["recip_used"] = tev
                if STOP_AFTER == "p1_av0":
                    return
        t_oT = (V.sem, V.cnt)
        if STOP_AFTER == "p1_mha":
            return
        T.wait(t_oT)
        for d2 in range(8):
            bk = 2 + (d2 % 2)
            bank = PS[bk]
            guard(T, bk)
            for j in range(8):
                T.op("matmul", bank[:, 0:TT_],
                     wB[:, j * 1024 + d2 * 128:j * 1024 + (d2 + 1) * 128],
                     oT_t[:, j * TT_:(j + 1) * TT_],
                     start=(j == 0), stop=False, skip_group_check=True)
            T.op("matmul", bank[:, 0:TT_],
                 bias_b[0:1, 3072 + d2 * 128:3072 + (d2 + 1) * 128], onesb[0:1, 0:TT_],
                 start=False, stop=True, skip_group_check=True)
            tmm = T.tok()
            V.wait(tmm)
            V.op("tensor_tensor", preT[:, d2 * TT_:(d2 + 1) * TT_],
                 AP3(bank, 0, [[512, 128], [1, 16], [32, 16], [16, 2]]),
                 xnTf[:, d2 * TT_:(d2 + 1) * TT_], ALU.add)
            release(bk, V.tok())
        t_pre = (V.sem, V.cnt)
        zps = PS[0]
        T.wait(t_pre)
        guard(T, 0)
        for j in range(8):
            T.op("matmul", zps[0:10, 0:TT_], gvec[:, 10 * j:10 * (j + 1)],
                 preT[:, j * TT_:(j + 1) * TT_],
                 start=(j == 0), stop=(j == 7), skip_group_check=True)
        t_z1 = T.tok()
        guard(T, 5)
        S.wait(t_pre)
        prev_mm = None
        for j in range(8):
            half = j % 2
            if prev_mm is not None and j >= 2:
                S.wait(prev_prev_mm if j >= 2 else None)
            S.op("activation", sq2[:, half * TT_:(half + 1) * TT_],
                 preT[:, j * TT_:(j + 1) * TT_], AF.Square)
            tS = S.tok()
            T.wait(tS)
            if j == 0:
                T.wait(t_z1)
            T.op("matmul", PS[5][0:1, 0:TT_], misc[:, 1:2], sq2[:, half * TT_:(half + 1) * TT_],
                 start=(j == 0), stop=(j == 7), skip_group_check=True)
            prev_prev_mm = prev_mm
            prev_mm = T.tok()
        t_z2 = prev_mm
        V.wait(t_z2, t_z1)
        V.op("memset", strow[0:16, 0:TT_], 0.0)
        V.op("tensor_copy", strow[0:10, 0:TT_], zps[0:10, 0:TT_])
        V.op("tensor_copy", sqrow[0:1, 0:TT_], PS[5][0:1, 0:TT_])
        t_zev = V.tok()
        release(4, t_zev)
        release(5, t_zev)
        T.wait(t_zev)
        for sub in range(4):
            guard(T, "PT")
            T.op("transpose", PT[0:128, 0:16], strow[0:16, sub * 128:(sub + 1) * 128],
                 identf[0:16, 0:16])
            tT = T.tok()
            V.wait(tT)
            V.op("tensor_copy", ztk[:, sub * 16:sub * 16 + 16], PT[0:128, 0:16])
            release("PT", V.tok())
            guard(T, "PT")
            T.op("transpose", PT[0:128, 0:1], sqrow[0:1, sub * 128:(sub + 1) * 128],
                 identf[0:1, 0:1])
            tT = T.tok()
            V.wait(tT)
            V.op("tensor_copy", ztk[:, sub * 16 + 10:sub * 16 + 11], PT[0:128, 0:1])
            release("PT", V.tok())
        t_ztk = (V.sem, V.cnt)
        V.wait(t_csgb)
        for sub in range(4):
            g = 4 * ti + sub
            zc = ztk[:, sub * 16:sub * 16 + 16]
            za = zarr[:, 16 * g:16 * g + 16]
            V.op("tensor_scalar", sct[:, 0:1], zc[:, 8:9], 1.0 / D, None, ALU.mult)
            V.op("tensor_scalar", sct[:, 1:2], zc[:, 10:11], 1.0 / D, None, ALU.mult)
            V.op("tensor_tensor", sct[:, 2:3], sct[:, 0:1], sct[:, 0:1], ALU.mult)
            V.op("tensor_tensor", sct[:, 1:2], sct[:, 1:2], sct[:, 2:3], ALU.subtract)
            V.op("tensor_scalar", sct[:, 1:2], sct[:, 1:2], 1e-5, None, ALU.add)
            V.op("reciprocal", sct[:, 2:3], sct[:, 1:2])
            tV = V.tok()
            S.wait(tV)
            S.op("activation", sct[:, 3:4], sct[:, 2:3], AF.Sqrt)
            tS = S.tok()
            V.wait(tS)
            V.op("tensor_scalar", sct[:, 4:12], misc[:, 16:24], sct[:, 0:1], None, ALU.mult)
            V.op("tensor_tensor", za[:, 0:8], zc[:, 0:8], sct[:, 4:12], ALU.subtract)
            V.op("tensor_scalar", za[:, 0:8], za[:, 0:8], sct[:, 3:4], None, ALU.mult)
            V.op("tensor_tensor", za[:, 0:8], za[:, 0:8], misc[:, 24:32], ALU.add)
            V.op("reduce_max", za[:, 8:9], za[:, 0:8], axis=AX.X)
            V.op("tensor_scalar", sct[:, 4:5], za[:, 8:9], -1.0, None, ALU.mult)
            tV = V.tok()
            S.wait(tV)
            S.op("activation", sct[:, 5:13], za[:, 0:8], AF.Exp, bias=sct[:, 4:5],
                 accum_out=sct[:, 13:14])
            tS = S.tok()
            V.wait(tS)
            V.op("reciprocal", za[:, 9:10], sct[:, 13:14])
            V.op("tensor_scalar", iseq_sb[:, 8 * g:8 * g + 8], za[:, 0:8], za[:, 8:9],
                 None, ALU.is_ge)
            V.op("tensor_scalar", sct[:, 4:12], misc[:, 8:16], -8.0, None, ALU.add)
            V.op("tensor_tensor", sct[:, 4:12], sct[:, 4:12], iseq_sb[:, 8 * g:8 * g + 8],
                 ALU.mult)
            V.op("tensor_reduce", za[:, 10:11], sct[:, 4:12], axis=AX.X, op=ALU.min)
            V.op("tensor_scalar", za[:, 10:11], za[:, 10:11], 8.0, None, ALU.add)
            V.op("tensor_scalar", iseq_sb[:, 8 * g:8 * g + 8], misc[:, 8:16],
                 za[:, 10:11], None, ALU.is_equal)
        t_route = V.tok()
        if STOP_AFTER == "p1_route":
            return
        for sub in range(4):
            zc = ztk[:, sub * 16:sub * 16 + 16]
            V.op("tensor_scalar", st4[:, sub:sub + 1], zc[:, 8:9], -1.0 / D, None, ALU.mult)
        for sub in range(4):
            zc = ztk[:, sub * 16:sub * 16 + 16]
            V.op("tensor_scalar", sct[:, 0:1], zc[:, 10:11], 1.0 / D, None, ALU.mult)
            V.op("tensor_tensor", sct[:, 1:2], st4[:, sub:sub + 1], st4[:, sub:sub + 1], ALU.mult)
            V.op("tensor_tensor", sct[:, 0:1], sct[:, 0:1], sct[:, 1:2], ALU.subtract)
            V.op("tensor_scalar", sct[:, 0:1], sct[:, 0:1], 1e-5, None, ALU.add)
            V.op("reciprocal", sct[:, 1:2], sct[:, 0:1])
            tV = V.tok()
            S.wait(tV)
            S.op("activation", st4[:, 4 + sub:5 + sub], sct[:, 1:2], AF.Sqrt)
            tS = S.tok()
            V.wait(tS)
            V.op("tensor_tensor", st4[:, 8 + sub:9 + sub], st4[:, sub:sub + 1],
                 st4[:, 4 + sub:5 + sub], ALU.mult)
        tV = V.tok()
        T.wait(tV)
        for (cb, drow) in ((4, 2), (8, 3)):
            guard(T, "PT")
            guard(T, "strow")
            T.op("transpose", PT[0:4, 0:128], st4[:, cb:cb + 4], identf[:, :])
            tT = T.tok()
            V.wait(tT)
            V.op("tensor_copy", strow[0:4, 0:128], PT[0:4, 0:128])
            tV2 = V.tok()
            release("PT", tV2)
            Y.wait(tV2)
            Y.dma(out=AP3(stat_rows, drow * TPC + ti * TT_, [[128, 4], [1, 128]]),
                  in_=strow[0:4, 0:128])
            release("strow", Y.dtok())
        t_str2 = Y.dtok()
        S.wait(t_str2)
        S.dma(out=bc_a[:, :], in_=stat_rows[2:3, ti * TT_:(ti + 1) * TT_].broadcast_to([128, TT_]))
        S.dma(out=bc_b[:, :], in_=stat_rows[3:4, ti * TT_:(ti + 1) * TT_].broadcast_to([128, TT_]))
        t_bc2 = S.tok()
        V.wait(t_bc2)
        for j in range(8):
            V.op("tensor_tensor", y1Tt[:, j * TT_:(j + 1) * TT_],
                 preT[:, j * TT_:(j + 1) * TT_], bc_a[:, :], ALU.mult)
            V.op("tensor_tensor", y1Tt[:, j * TT_:(j + 1) * TT_],
                 y1Tt[:, j * TT_:(j + 1) * TT_], bc_b[:, :], ALU.add)
            V.op("tensor_scalar", y1Tt[:, j * TT_:(j + 1) * TT_],
                 y1Tt[:, j * TT_:(j + 1) * TT_], lnv[:, j:j + 1], lnv[:, 8 + j:9 + j],
                 ALU.mult, ALU.add)
        t_y1 = V.tok()
        t_y1d = None
        for sub in range(4):
            for j in range(8):
                if sub == 0 and j == 0:
                    T.wait(t_y1)
                guard(T, "PT")
                T.op("transpose", PT[:, :], y1Tt[:, j * TT_ + sub * 128:j * TT_ + (sub + 1) * 128],
                     identf[:, :])
                tT = T.tok()
                V.wait(tT)
                if j == 0 and t_y1d is not None:
                    V.wait(t_y1d)
                V.op("tensor_copy", y1g[:, j * 128:(j + 1) * 128], PT[:, :])
                V.op("tensor_copy", y1gb[:, j * 128:(j + 1) * 128], PT[:, :])
                release("PT", V.tok())
            t_y1t = (V.sem, V.cnt)
            Y.wait(t_y1t)
            Y.dma(out=tokrow_ap(y1tok_f, ti, sub, D), in_=y1g[:, :])
            Y.dma(out=tokrow_ap(y1tok_b, ti, sub, D), in_=y1gb[:, :])
            t_y1d = Y.dtok()
        return t_y1d

    t_p1 = None
    for ti in range(NTi):
        t_p1 = p1_tile(ti)
        if STOP_AFTER in ("p1_stats", "p1_xn", "p1_qkv", "p1_sc", "p1_msk", "p1_den", "p1_av0", "p1_mha", "p1_route"):
            break


    RG = [list(range(NCORES))]

    def emit_p2():
        onehotE = x0t
        cumE = preT
        for g in range(NG):
            ti, sub = g // 4, g % 4
            guard(T, "PT")
            T.op("transpose", PT[0:8, 0:128], iseq_sb[:, 8 * g:8 * g + 8],
                 identf[:, :])
            tT = T.tok()
            V.wait(tT)
            V.op("tensor_copy",
                 AP3(onehotE, 4 * sub * 256 + ti * 32, [[4 * D, 8], [256, 4], [1, 32]]),
                 PT[0:8, 0:128])
            release("PT", V.tok())
        V.op("memset", sqbuf[0:8, 0:256], 1.0)
        for l in range(B):
            V.op("tensor_tensor_scan",
                 AP3(cumE, l * 256, [[8 * TT_, 8], [1, 256]]),
                 sqbuf[0:8, 0:256],
                 AP3(onehotE, l * 256, [[4 * D, 8], [1, 256]]),
                 0.0, ALU.mult, ALU.add)
        t_cum = V.tok()
        V.op("tensor_copy", sct[0:8, 0:16], AP3(cumE, 255, [[8 * TT_, 8], [256, 16]]))
        tV = V.tok()
        guard(T, "PT")
        T.wait(tV)
        T.op("transpose", PT[0:16, 0:8], sct[0:8, 0:16], identf[0:8, 0:8])
        tT = T.tok()
        V.wait(tT)
        V.op("tensor_copy", strow[0:16, 0:8], PT[0:16, 0:8])
        tV = V.tok()
        release("PT", tV)
        Y.wait(tV)
        Y.dma(out=cnt_in, in_=strow[0:16, 0:8])
        t_ci = Y.dtok()
        G.wait(t_ci)
        G.op("collective_compute", "AllGather", ALU.bypass, replica_groups=RG,
             ins=[cnt_in.opt()], outs=[cnt_out.opt()])
        t_ag = G.tok()
        G.wait(t_ag)
        G.dma(out=cntb[:, :], in_=cnt_out)
        t_cb = G.dtok()
        offp = PS[1]
        T.wait(t_cb)
        guard(T, 1)
        first = True
        for sub in range(4):
            for e in range(E):
                T.op("matmul", offp[:, sub * 8 + e:sub * 8 + e + 1],
                     maskP_b[:, sub * 128:(sub + 1) * 128], cntb[:, e:e + 1],
                     start=first, stop=(sub == 3 and e == E - 1), skip_group_check=True)
                first = False
        t_off = T.tok()
        V.wait(t_off)
        V.op("tensor_copy", ztk[:, 0:32], offp[:, 0:32])
        t_offe = V.tok()
        release(1, t_offe)
        for g in range(NG):
            ti, sub = g // 4, g % 4
            za = zarr[:, 16 * g:16 * g + 16]
            iq = iseq_sb[:, 8 * g:8 * g + 8]
            V.wait(t_cum)
            V.op("tensor_copy", bc_a[0:8, 0:128],
                 AP3(cumE, 4 * sub * 256 + ti * 32, [[8 * TT_, 8], [256, 4], [1, 32]]))
            tV = V.tok()
            guard(T, "PT")
            T.wait(tV)
            T.op("transpose", PT[0:128, 0:8], bc_a[0:8, 0:128], identf[0:8, 0:8])
            tT = T.tok()
            V.wait(tT)
            V.op("tensor_copy", bc_b[:, 0:8], PT[0:128, 0:8])
            release("PT", V.tok())
            V.op("tensor_tensor", bc_b[:, 8:16], bc_b[:, 0:8], ztk[:, sub * 8:sub * 8 + 8],
                 ALU.add)
            V.op("tensor_tensor", bc_b[:, 8:16], bc_b[:, 8:16], iq, ALU.mult)
            V.op("reduce_sum", za[:, 11:12], bc_b[:, 8:16], axis=AX.X)
            V.op("tensor_scalar", za[:, 11:12], za[:, 11:12], -1.0, None, ALU.add)
            V.op("tensor_scalar", za[:, 12:13], za[:, 11:12], float(CAP), None, ALU.is_lt)
            V.op("tensor_tensor", za[:, 15:16], za[:, 9:10], za[:, 12:13], ALU.mult)
            V.op("tensor_scalar", bc_b[:, 16:17], za[:, 10:11], float(CAP), None, ALU.mult)
            V.op("tensor_tensor", bc_b[:, 16:17], bc_b[:, 16:17], za[:, 11:12], ALU.add)
            V.op("tensor_scalar", bc_b[:, 17:18], bc_b[:, 16:17], misc[:, 36:37], None,
                 ALU.subtract)
            V.op("tensor_tensor", bc_b[:, 17:18], bc_b[:, 17:18], za[:, 12:13], ALU.mult)
            V.op("tensor_scalar", za[:, 13:14], bc_b[:, 17:18], misc[:, 36:37], None,
                 ALU.add)
            V.op("tensor_scalar", za[:, 14:15], za[:, 13:14], float(NCORES * CAP - 1),
                 None, ALU.min)
            V.op("tensor_copy", slot_i[:, g:g + 1], za[:, 13:14])
            V.op("tensor_copy", slotg_i[:, g:g + 1], za[:, 14:15])
        return V.tok()

    t_p2 = emit_p2()
    if STOP_AFTER == "p2":
        env = dict(locals())
        if DBG_HOOK[0] is not None:
            DBG_HOOK[0](p, env)
        Y.wait(Y.dtok(), S.dtok(), G.dtok(), t_p2)
        Y.op("nop")
        p.play()
        es1a.close()
        es1.close()
        es.close()
        return nc, dram

    def emit_p4a():
        Y.wait(bg.get("zr"))
        Y.dma(out=dram["zero_d"], in_=zr_row[0:1, :])
        tz0 = Y.dtok()
        G.wait(tz0)
        NR = NCORES * CAP + 128
        step = NR // 8
        for k in range(8):
            G.dma(out=rs_in[k * step:(k + 1) * step, :],
                  in_=dram["zero_d"].broadcast_to([step, D]))
        t_zero = G.dtok()
        for g in range(NG):
            ti, sub = g // 4, g % 4
            Y.wait(bg.get("y1gb") or t_p1)
            Y.dma(out=y1gb[:, :], in_=tokrow_ap(y1tok_b, ti, sub, D))
            tld = Y.dtok()
            G.wait(tld, t_zero, t_p2)
            G.dma_op("indirect_dma_start", out=rs_in,
                     out_offset=IndirectOffsetOnAxis(slot_i[:, g:g + 1], 0),
                     in_=y1gb[:, :], in_offset=None)
            bg["y1gb"] = G.dtok()
        t_scat = G.dtok()
        G.wait(t_scat)
        G.op("collective_compute", "ReduceScatter", ALU.add, replica_groups=RG,
             ins=[rs_in[0:NCORES * CAP, :].opt()], outs=[rs_out.opt()])
        return G.tok()

    t_rs = emit_p4a()
    if STOP_AFTER == "p4a":
        env = dict(locals())
        if DBG_HOOK[0] is not None:
            DBG_HOOK[0](p, env)
        Y.wait(*all_dtoks(Y), *all_dtoks(S), *all_dtoks(G), t_rs)
        Y.op("nop")
        p.play()
        es1a.close()
        es1.close()
        es.close()
        return nc, dram
    es1a.close()

    def emit_p3():
        G.wait(bg.get("wA"))
        for j in range(8):
            G.dma(out=wA[:, j * 1024:(j + 1) * 1024], in_=dram["w_q"][j * 128:(j + 1) * 128, :])
            G.dma(out=wA[:, 8192 + j * 1024:8192 + (j + 1) * 1024],
                  in_=dram["w_k"][j * 128:(j + 1) * 128, :])
            G.dma(out=wA[:, 16384 + j * 1024:16384 + (j + 1) * 1024],
                  in_=dram["w_v"][j * 128:(j + 1) * 128, :])
            G.dma(out=wB[:, j * 1024:(j + 1) * 1024], in_=dram["w_ao"][j * 128:(j + 1) * 128, :])
        G.dma(out=bias_b[0:1, 0:1024], in_=dram["b_q"])
        G.dma(out=bias_b[0:1, 1024:2048], in_=dram["b_k"])
        G.dma(out=bias_b[0:1, 2048:3072], in_=dram["b_v"])
        G.dma(out=bias_b[0:1, 3072:4096], in_=dram["b_ao"])
        t_w = G.dtok()
        xnl = es1.enter_context(nc.sbuf_tensor("xnl", [128, 8 * 2048], BF16))
        dens = bc_b
        for pb in range(2):
            if pb > 0:
                Y.wait((T.sem, T.cnt))
            for j in range(8):
                Y.dma(out=xnl[:, j * 2048:(j + 1) * 2048],
                      in_=AP3(xnT_d, j * 128 * TPC + pb * 2048, [[TPC, 128], [1, 2048]]))
            t_xnl = Y.dtok()
            for l in range(B):
                patch_ap = lambda j, off=0, cnt=128: AP3(
                    xnl, j * 2048 + l * 32 + off, [[8 * 2048, 128], [512, 4], [1, 32]])
                T.wait(t_xnl, t_w)
                for (wofs, dst, bofs) in ((0, qT_t, 0), (8192, kT_t, 1024), (16384, vtok, 2048)):
                    for fc in range(8):
                        bank = PS[fc % 2]
                        guard(T, fc % 2)
                        for j in range(8):
                            T.op("matmul", bank[:, 0:128],
                                 wA[:, wofs + j * 1024 + fc * 128:wofs + j * 1024 + (fc + 1) * 128],
                                 patch_ap(j),
                                 start=(j == 0), stop=False, skip_group_check=True)
                        T.op("matmul", bank[:, 0:128],
                             bias_b[0:1, bofs + fc * 128:bofs + (fc + 1) * 128],
                             onesb[0:1, 0:128], start=False, stop=True,
                             skip_group_check=True)
                        tmm = T.tok()
                        V.wait(tmm)
                        V.op("tensor_copy", dst[:, fc * 128:(fc + 1) * 128], bank[:, 0:128])
                        release(fc % 2, V.tok())
                if STOP_AFTER == "p3a":
                    return Y.dtok()
                for h in (0, 2, 4, 6, 8, 10, 12, 14, 1, 3, 5, 7, 9, 11, 13, 15):
                    sg = h // 2 + 8 * (h % 2)
                    bk = 2 + sg // 4
                    col = sg % 4
                    hb = (h % 2) * 64
                    ch = (h // 2) * 128
                    if col == 0:
                        guard(T, bk)
                        if h % 2 == 0:
                            T.op("matmul", PS[bk][:, 0:512], zerosb[:, :],
                                 kT_t[:, 0:512], start=True, stop=False,
                                 skip_group_check=True)
                        else:
                            T.op("matmul", PS[bk][:, 0:512], zerosb[64:128, :],
                                 qT_t[64:128, 0:512], start=True, stop=False,
                                 skip_group_check=True)
                    T.op("matmul", PS[bk][:, col * 128:col * 128 + 128],
                         qT_t[hb:hb + 64, ch:ch + 128], kT_t[hb:hb + 64, ch:ch + 128],
                         start=False, stop=(col == 3), skip_group_check=True)
                    tsc = T.tok()
                    if col == 3:
                        S.wait(tsc)
                        S.op("activation", expst[:, (bk - 2) * 512:(bk - 1) * 512],
                             PS[bk][:, :], AF.Exp, bias=misc[:, 35:36], scale=1.0 / 8.0)
                        release(bk, S.tok())
                tse = S.tok()
                if STOP_AFTER == "p3b":
                    return Y.dtok()
                V.wait(tse)
                for b4 in range(4):
                    V.op("tensor_reduce",
                         AP3(dens, b4 * 4, [[TT_, 128], [1, 4]]),
                         AP3(expst, b4 * 512, [[2048, 128], [128, 4], [1, 128]]),
                         axis=AX.X, op=ALU.add)
                V.op("reciprocal", sct[:, 0:16], dens[:, 0:16])
                t_den = V.tok()
                for h in range(PH):
                    V.op("tensor_scalar", expst[:, h * 128:(h + 1) * 128],
                         expst[:, h * 128:(h + 1) * 128], sct[:, h:h + 1], None, ALU.mult)
                t_nrm = V.tok()
                for h in range(PH):
                    guard(T, "PTB")
                    T.wait(t_nrm if h == 0 else None)
                    T.op("transpose", PTB[:, :], expst[:, h * 128:(h + 1) * 128],
                         ident[:, :])
                    tT = T.tok()
                    V.wait(tT)
                    V.op("tensor_copy", xnTb[:, h * 128:(h + 1) * 128], PTB[:, :])
                    release("PTB", V.tok())
                t_at = V.tok()
                for j in range(8):
                    guard(T, "PTB")
                    T.op("transpose", PTB[:, :], vtok[:, j * 128:(j + 1) * 128],
                         ident[:, :])
                    tT = T.tok()
                    V.wait(tT)
                    V.op("tensor_copy", kT_t[:, j * 128:(j + 1) * 128], PTB[:, :])
                    release("PTB", V.tok())
                t_vtk = V.tok()
                if STOP_AFTER == "p3d":
                    return Y.dtok()
                T.wait(t_at, t_vtk)
                for h in range(PH):
                    bk = 2 + (h // 4)
                    if h % 4 == 0:
                        guard(T, bk)
                    rb = (h % 2) * 64
                    cb = ((h % 4) // 2) * 128
                    dcol = (h // 2) * 128 + (h % 2) * 64
                    sg = h // 2 + 8 * (h % 2)
                    T.op("matmul", PS[bk][rb:rb + 64, cb:cb + 128],
                         kT_t[:, dcol:dcol + 64], xnTb[:, sg * 128:(sg + 1) * 128],
                         start=(h % 4 < 2), stop=(h % 4 >= 2), skip_group_check=True)
                    tav = T.tok()
                    if h % 4 == 3:
                        V.wait(tav)
                        V.op("tensor_copy", vtok[:, 2048 + (h // 4) * 256:2048 + (h // 4 + 1) * 256],
                             PS[bk][:, 0:256])
                        release(bk, V.tok())
                t_oh = V.tok()
                if STOP_AFTER == "p3e":
                    return Y.dtok()
                T.wait(t_oh)
                for half in range(2):
                    for par in range(2):
                        bank = PS[2 + 2 * half + par]
                        guard(T, 2 + 2 * half + par)
                        hs = [h for h in range(PH) if h % 2 == par]
                        for i, h in enumerate(hs):
                            hb2 = par * 64
                            ohc = 2048 + (h // 4) * 256 + ((h % 4) // 2) * 128
                            T.op("matmul", bank[:, 0:512],
                                 vtok[hb2:hb2 + 64, ohc:ohc + 128],
                                 wB[hb2:hb2 + 64, (h // 2) * 1024 + half * 512:(h // 2) * 1024 + (half + 1) * 512],
                                 start=(i == 0),
                                 stop=(par == 1 and i == len(hs) - 1),
                                 skip_group_check=True)
                        if par == 0:
                            T.op("matmul", bank[:, 0:512],
                                 onesb[0:1, 0:128],
                                 bias_b[0:1, 3072 + half * 512:3072 + (half + 1) * 512],
                                 start=False, stop=True, skip_group_check=True)
                    tmm = T.tok()
                    V.wait(tmm, bg.get("sstage"))
                    V.op("tensor_copy", xnTf[:, half * 512:(half + 1) * 512],
                         PS[2 + 2 * half][:, 0:512])
                    release(2 + 2 * half, V.tok())
                    V.op("tensor_tensor", xnTf[:, half * 512:(half + 1) * 512],
                         xnTf[:, half * 512:(half + 1) * 512],
                         PS[3 + 2 * half][:, 0:512], ALU.add)
                    release(3 + 2 * half, V.tok())
                t_s = V.tok()
                Y.wait(t_s)
                Y.dma(out=dram["s_tok"][(l * NPC + pb * 128):(l * NPC + pb * 128 + 128), :],
                      in_=xnTf[:, 0:1024])
                bg["sstage"] = Y.dtok()
        return Y.dtok()

    t_p3 = emit_p3()
    if STOP_AFTER is not None and STOP_AFTER.startswith("p3"):
        env = dict(locals())
        if DBG_HOOK[0] is not None:
            DBG_HOOK[0](p, env)
        Y.wait(Y.dtok(), S.dtok(), G.dtok(), t_rs)
        Y.op("nop")
        p.play()
        es1.close()
        es.close()
        return nc, dram

    es1.close()
    esM = ExitStack()

    def sbM(name, shape, dtype):
        return esM.enter_context(nc.sbuf_tensor(name, shape, dtype))

    dispT = sbM("dispT", [128, 8 * 256], BF16)
    hTb = sbM("hTb", [128, 4 * 256], BF16)
    yTb = sbM("yTb", [128, 8 * 256], BF16)
    w1blk = sbM("w1blk", [128, 8 * 512], BF16)
    w2blk = sbM("w2blk", [128, 4 * 1024], BF16)
    bias2 = sbM("bias2", [128, 8], F32)
    ytile = sbM("ytile", [128, D], BF16)
    a5 = sbM("a5", [128, D], F32)
    moet = sbM("moet", [128, D], BF16)
    w5 = sbM("w5", [128, D], F32)
    st5 = sbM("st5", [128, 16], F32)

    def emit_p4b(t_p3):
        G.wait((T.sem, T.cnt), t_p3)
        S.wait(t_p3)
        Y.wait(t_p3)
        G.dma(out=dram["w1b_d"], in_=dram["w1e"])
        G.dma(out=dram["w2b_d"], in_=dram["w2e"])
        G.dma(out=bias_b[0:1, 0:FF], in_=dram["b1e"])
        t_wc = G.dtok()
        for j in range(8):
            Y.dma(out=bias2[:, j:j + 1], in_=AP3(dram["b2e"], j * 128, [[1, 128], [1, 1]]))
        t_b2 = Y.dtok()
        G.wait(t_rs)
        t_data = G.dtok()
        for st in range(16):
            S.wait(t_rs, bg.get("dispT"), t_p3)
            for j in range(8):
                S.dma(out=dispT[:, j * 256:(j + 1) * 256],
                      in_=dram["rs_out"][st * 256:(st + 1) * 256, j * 128:(j + 1) * 128],
                      transpose=True)
            t_dt = S.dtok()
            for fb in range(8):
                Y.wait(bg.get("w1blk"), t_wc, t_p3)
                for j in range(8):
                    Y.dma(out=w1blk[:, j * 512:(j + 1) * 512],
                          in_=dram["w1b_d"][j * 128:(j + 1) * 128, fb * 512:(fb + 1) * 512])
                for j4 in range(4):
                    Y.dma(out=w2blk[:, j4 * 1024:(j4 + 1) * 1024],
                          in_=dram["w2b_d"][fb * 512 + j4 * 128:fb * 512 + (j4 + 1) * 128, :])
                t_wb = Y.dtok()
                T.wait(t_wb, t_dt)
                for fc in range(4):
                    bank = PS[fc % 2]
                    guard(T, fc % 2)
                    for j in range(8):
                        T.op("matmul", bank[:, 0:256],
                             w1blk[:, j * 512 + fc * 128:j * 512 + (fc + 1) * 128],
                             dispT[:, j * 256:(j + 1) * 256],
                             start=(j == 0), stop=False, skip_group_check=True)
                    T.op("matmul", bank[:, 0:256],
                         bias_b[0:1, fb * 512 + fc * 128:fb * 512 + (fc + 1) * 128],
                         onesb[0:1, 0:256], start=False, stop=True, skip_group_check=True)
                    tmm = T.tok()
                    V.wait(tmm)
                    V.op("tensor_copy", hTb[:, fc * 256:(fc + 1) * 256], bank[:, 0:256])
                    release(fc % 2, V.tok())
                t_h = V.tok()
                T.wait(t_h)
                if fb == 0:
                    for bk in range(2, 6):
                        guard(T, bk)
                        T.op("matmul", PS[bk][:, 0:512], zerosb[:, :],
                             dispT[:, 0:512], start=True, stop=False,
                             skip_group_check=True)
                for d2 in range(8):
                    bank = PS[2 + d2 // 2]
                    for j4 in range(4):
                        T.op("matmul", bank[:, (d2 % 2) * 256:(d2 % 2) * 256 + 256],
                             w2blk[:, j4 * 1024 + d2 * 128:j4 * 1024 + (d2 + 1) * 128],
                             hTb[:, j4 * 256:(j4 + 1) * 256],
                             start=False,
                             stop=(fb == 7 and j4 == 3 and d2 % 2 == 1),
                             skip_group_check=True)
                bg["w1blk"] = T.tok()
            t_mm2 = (T.sem, T.cnt)
            for d2 in range(8):
                bank = PS[2 + d2 // 2]
                V.wait(t_mm2, bg.get("yTb"), t_b2)
                V.op("tensor_scalar", yTb[:, d2 * 256:(d2 + 1) * 256],
                     bank[:, (d2 % 2) * 256:(d2 % 2) * 256 + 256], bias2[:, d2:d2 + 1],
                     None, ALU.add)
                if d2 % 2 == 1:
                    release(2 + d2 // 2, V.tok())
            t_y = V.tok()
            Y.wait(t_y)
            for j in range(8):
                Y.dma(out=AP3(dram["yexpT_d"], j * 128 * CAP + st * 256, [[CAP, 128], [1, 256]]),
                      in_=yTb[:, j * 256:(j + 1) * 256])
            bg["yTb"] = Y.dtok()
            bg["dispT"] = (T.sem, T.cnt)
        t_all = Y.dtok()
        for gr in range(NG):
            S.wait(t_all, bg.get("ytile"))
            S.dma(out=ytile[:, :], in_=dram["yexpT_d"][:, gr * 128:(gr + 1) * 128],
                  transpose=True)
            tt = S.dtok()
            Y.wait(tt)
            Y.dma(out=dram["yret_in"][gr * 128:(gr + 1) * 128, :], in_=ytile[:, :])
            bg["ytile"] = Y.dtok()
        t_in = Y.dtok()
        G.wait(t_in)
        G.op("collective_compute", "AllGather", ALU.bypass, replica_groups=RG,
             ins=[dram["yret_in"].opt()], outs=[dram["yret_out"].opt()])
        return G.tok()

    t_moe = emit_p4b(t_p3)

    def emit_p5():
        for g in range(NG):
            ti, sub = g // 4, g % 4
            za = zarr[:, 16 * g:16 * g + 16]
            Y.wait(bg.get("a5"))
            Y.dma(out=a5[:, :], in_=tokrow_ap(y1tok_f, ti, sub, D))
            t_a = Y.dtok()
            G.wait(t_moe, bg.get("moet"))
            G.dma_op("indirect_dma_start", out=moet[:, :], out_offset=None,
                     in_=dram["yret_out"],
                     in_offset=IndirectOffsetOnAxis(slotg_i[:, g:g + 1], 0))
            t_m = G.dtok()
            V.wait(t_a, t_m)
            V.op("scalar_tensor_tensor", w5[:, :], moet[:, :], za[:, 15:16], a5[:, :],
                 ALU.mult, ALU.add)
            bg["moet"] = V.tok()
            bg["a5"] = bg["moet"]
            V.op("reduce_sum", st5[:, 0:1], w5[:, :], axis=AX.X)
            tV = V.tok()
            S.wait(tV)
            S.op("activation", a5[:, :], w5[:, :], AF.Square, accum_out=st5[:, 1:2])
            tS = S.tok()
            V.wait(tS)
            V.op("tensor_scalar", st5[:, 2:3], st5[:, 0:1], -1.0 / D, None, ALU.mult)
            V.op("tensor_tensor", st5[:, 3:4], st5[:, 2:3], st5[:, 2:3], ALU.mult)
            V.op("tensor_scalar", st5[:, 4:5], st5[:, 1:2], 1.0 / D, None, ALU.mult)
            V.op("scalar_tensor_tensor", st5[:, 5:6], st5[:, 4:5], 1e-5, st5[:, 3:4],
                 ALU.add, ALU.subtract)
            V.op("reciprocal", st5[:, 6:7], st5[:, 5:6])
            tV = V.tok()
            S.wait(tV)
            S.op("activation", st5[:, 7:8], st5[:, 6:7], AF.Sqrt)
            tS = S.tok()
            V.wait(tS)
            V.op("tensor_scalar", w5[:, :], w5[:, :], st5[:, 2:3], st5[:, 7:8],
                 ALU.add, ALU.mult)
            V.op("tensor_tensor", w5[:, :], w5[:, :], g2row[:, 0:D], ALU.mult)
            V.op("tensor_tensor", w5[:, :], w5[:, :], g2row[:, D:2 * D], ALU.add)
            tV = V.tok()
            S.wait(tV)
            S.op("activation", a5[:, :], w5[:, :], AF.Silu)
            tS = S.tok()
            Y.wait(tS)
            Y.dma(out=tokrow_ap(g_tok, ti, sub, D), in_=a5[:, :])
            bg["a5"] = Y.dtok()
        return Y.dtok()

    t_p5 = emit_p5()
    if STOP_AFTER == "p5":
        env = dict(locals())
        if DBG_HOOK[0] is not None:
            DBG_HOOK[0](p, env)
        Y.wait(Y.dtok(), S.dtok(), G.dtok(), t_p5, t_p3)
        Y.op("nop")
        p.play()
        esM.close()
        es.close()
        return nc, dram
    esM.close()
    esP6 = ExitStack()

    def sbP6(name, shape, dtype):
        return esP6.enter_context(nc.sbuf_tensor(name, shape, dtype))

    x1t = sbP6("x1t", [128, D], F32)
    s6 = sbP6("s6", [128, D], F32)
    g6 = sbP6("g6", [128, D], F32)
    xn6 = sbP6("xn6", [128, D], F32)
    xn6b = sbP6("xn6b", [128, D], BF16)
    st6 = sbP6("st6", [128, 16], F32)
    q6 = sbP6("q6", [128, 128], F32)
    xq6 = sbP6("xq6", [128, D], BF16)
    halo_i = sbP6("halo_i", [128, 8], I32)

    def quant_apply(src, qst, dst, width, pitch, qpitch):
        nb = width // 32
        V.op("tensor_reduce", qst[:, 0:nb],
             AP3(src, 0, [[pitch, 128], [32, nb], [1, 32]]),
             axis=AX.X, op=ALU.max, apply_absolute_value=True)
        V.op("tensor_scalar", qst[:, 0:nb], qst[:, 0:nb], TINY, None, ALU.max)
        tV = V.tok()
        S.wait(tV)
        S.op("activation", qst[:, nb:2 * nb], qst[:, 0:nb], AF.Ln)
        tS = S.tok()
        V.wait(tS)
        V.op("tensor_scalar", qst[:, nb:2 * nb], qst[:, nb:2 * nb], INV_LN2,
             -0.5, ALU.mult, ALU.add)
        V.op("tensor_scalar", qst[:, nb:2 * nb], qst[:, nb:2 * nb], MAGIC, MAGIC,
             ALU.add, ALU.subtract)
        tV = V.tok()
        S.wait(tV)
        S.op("activation", qst[:, 2 * nb:3 * nb], qst[:, nb:2 * nb], AF.Exp,
             scale=-LN2, bias=misc[:, 37:38])
        S.op("activation", qst[:, 3 * nb:4 * nb], qst[:, nb:2 * nb], AF.Exp,
             scale=LN2, bias=misc[:, 38:39])
        tS = S.tok()
        V.wait(tS)
        V.op("tensor_tensor", src[:, 0:width], src[:, 0:width],
             AP3(qst, 2 * nb, [[qpitch, 128], [1, nb], [0, 32]]), ALU.mult)
        V.op("tensor_scalar", src[:, 0:width], src[:, 0:width], MAGIC,
             MAGIC, ALU.add, ALU.subtract)
        V.op("tensor_tensor", dst[:, 0:width], src[:, 0:width],
             AP3(qst, 3 * nb, [[qpitch, 128], [1, nb], [0, 32]]), ALU.mult)
        return V.tok()

    def emit_p6a():
        for gr in range(NG):
            r0 = gr * 128
            Y.wait(bg.get("p6ld") or t_p5, bg.get("p6v"), t_p3)
            Y.dma(out=x1t[:, :], in_=x0tok[r0:r0 + 128, :])
            Y.dma(out=s6[:, :], in_=dram["s_tok"][r0:r0 + 128, :])
            Y.dma(out=g6[:, :], in_=dram["g_tok"][r0:r0 + 128, :])
            t_ld = Y.dtok()
            V.wait(t_ld, t_p5, bg.get("p6ld"))
            V.op("tensor_tensor", s6[:, :], s6[:, :], g6[:, :], ALU.mult)
            V.op("tensor_tensor", x1t[:, :], x1t[:, :], s6[:, :], ALU.add)
            t_x1 = V.tok()
            Y.wait(t_x1)
            Y.dma(out=dram["x1tok_d"][r0:r0 + 128, :], in_=x1t[:, :])
            V.op("reduce_sum", st6[:, 0:1], x1t[:, :], axis=AX.X)
            tV = V.tok()
            S.wait(tV, t_p5)
            S.op("activation", g6[:, :], x1t[:, :], AF.Square, accum_out=st6[:, 1:2])
            tS = S.tok()
            V.wait(tS)
            V.op("tensor_scalar", st6[:, 2:3], st6[:, 0:1], -1.0 / D, None, ALU.mult)
            V.op("tensor_tensor", st6[:, 3:4], st6[:, 2:3], st6[:, 2:3], ALU.mult)
            V.op("tensor_scalar", st6[:, 4:5], st6[:, 1:2], 1.0 / D, None, ALU.mult)
            V.op("scalar_tensor_tensor", st6[:, 5:6], st6[:, 4:5], 1e-5, st6[:, 3:4],
                 ALU.add, ALU.subtract)
            V.op("reciprocal", st6[:, 6:7], st6[:, 5:6])
            tV = V.tok()
            S.wait(tV)
            S.op("activation", st6[:, 7:8], st6[:, 6:7], AF.Sqrt)
            tS = S.tok()
            V.wait(tS)
            V.op("tensor_scalar", xn6[:, :], x1t[:, :], st6[:, 2:3], st6[:, 7:8],
                 ALU.add, ALU.mult)
            V.op("tensor_copy", xn6b[:, :], xn6[:, :])
            t_xn = V.tok()
            Y.wait(t_xn)
            Y.dma(out=dram["xn1tok_b"][r0:r0 + 128, :], in_=xn6b[:, :])
            if gr % 2 == 1:
                l = gr // 2
                t_xst = Y.dtok()
                G.wait(t_xst)
                G.dma(out=dram["tail_in"][l * HALO:(l + 1) * HALO, :],
                      in_=dram["xn1tok_b"][r0 + 64:r0 + 128, :])
            t_q = quant_apply(xn6, q6, xq6, D, D, 128)
            bg["p6v"] = t_q
            Y.wait(t_q)
            Y.dma(out=dram["xq_d"][r0:r0 + 128, :], in_=xq6[:, :])
            bg["p6ld"] = Y.dtok()
        t_all = all_dtoks(Y)
        for k in range(8):
            Y.dma(out=halo_i[:, k:k + 1],
                  in_=AP3(dram["halo_idx"], k * 128, [[1, 128], [1, 1]]))
        t_hi = Y.dtok()
        G.wait(*t_all)
        G.dma(out=dram["tail_in"][B * HALO:B * HALO + 16, :],
              in_=dram["zero_d"].broadcast_to([16, D]))
        t_z = G.dtok()
        G.wait(t_z)
        G.op("collective_compute", "AllGather", ALU.bypass, replica_groups=RG,
             ins=[dram["tail_in"].opt()], outs=[dram["tail_out"].opt()])
        t_ag = G.tok()
        for k in range(8):
            G.wait(t_ag, t_hi, bg.get("halo") or bg["p6ld"])
            G.dma_op("indirect_dma_start", out=xn6b[:, :], out_offset=None,
                     in_=dram["tail_out"],
                     in_offset=IndirectOffsetOnAxis(halo_i[:, k:k + 1], 0))
            th = G.dtok()
            Y.wait(th)
            Y.dma(out=dram["halo_d"][k * 128:(k + 1) * 128, :], in_=xn6b[:, :])
            bg["halo"] = Y.dtok()
        return all_dtoks(Y) + all_dtoks(G)

    t6a = emit_p6a()
    if STOP_AFTER == "p6a":
        env = dict(locals())
        if DBG_HOOK[0] is not None:
            DBG_HOOK[0](p, env)
        Y.wait(*all_dtoks(Y), *all_dtoks(S), *all_dtoks(G))
        Y.op("nop")
        p.play()
        esP6.close()
        es.close()
        return nc, dram

    esP6.close()
    esA = ExitStack()

    def sbA(name, shape, dtype):
        return esA.enter_context(nc.sbuf_tensor(name, shape, dtype))

    XPITCH = 8 * 8 * LW
    xnh = sbA("xnh", [128, XPITCH], BF16)
    tmpA = sbA("tmpA", [128, 2048], BF16)
    wpv = sbA("wpv", [128, 1024], F32)
    wsb = sbA("wsb", [128, 512], BF16)
    cwrow = sbA("cwrow", [4, 1024], F32)
    cwv = sbA("cwv", [128, 32], F32)
    cb8 = sbA("cb8", [128, 8], F32)
    bprime = sbA("bprime", [128, 2], F32)
    wkb = sbA("wkb", [128, 32 * 128], BF16)
    wlo = sbA("wlo", [64, D], F32)
    blorow = sbA("blorow", [1, D], F32)
    onesf = sbA("onesf", [1, 128], F32)
    dcol = sbA("dcol", [64, 8], F32)
    maskS = sbA("maskS", [64, SCW], F32)
    BbF = sbA("BbF", [128, SCW], F32)
    Cb = sbA("Cb", [64, SCW], F32)
    ub = sbA("ub", [64, SCW], F32)
    zb = sbA("zb", [64, SCW], F32)
    yscan = sbA("yscan", [64, SCW], F32)
    dbc = sbA("dbc", [64, SCW], F32)
    s1t = sbA("s1t", [128, D], F32)

    def emit_p6b_attn():
        tT0 = (T.sem, T.cnt)
        G.wait(tT0, *t6a)
        for j in range(8):
            G.dma(out=wpv[:, j * 128:(j + 1) * 128],
                  in_=dram["wp"][j * 128:(j + 1) * 128, :])
        G.dma(out=dram["ws_b"], in_=dram["ws"])
        t_wsb = G.dtok()
        G.wait(t_wsb)
        for j in range(8):
            G.dma(out=wsb[:, j * 64:(j + 1) * 64],
                  in_=dram["ws_b"][j * 128:(j + 1) * 128, :])
        G.dma(out=cwrow[0:4, :], in_=dram["conv_w"])
        t_w = G.dtok()
        Y.wait(tT0, *t6a)
        Y.dma(out=wlo[0:64, :], in_=dram["w_loT"])
        Y.dma(out=blorow[0:1, :], in_=dram["b_lo"])
        Y.dma(out=maskS[0:64, :], in_=AP3(dram["scan_mask"], 0, [[0, 64], [1, SCW]]))
        for j in range(8):
            Y.dma(out=cb8[:, j:j + 1],
                  in_=AP3(dram["conv_b"], j * 128, [[1, 128], [1, 1]]))
        Y.dma(out=bprime[:, 1:2], in_=AP3(dram["bp"], 0, [[1, 128], [1, 1]]))
        Y.dma(out=dcol[0:64, 1:2], in_=AP3(dram["a_log"], 0, [[1, 64], [1, 1]]))
        Y.dma(out=dcol[0:64, 2:3], in_=AP3(dram["dt_log"], 0, [[1, 64], [1, 1]]))
        Y.dma(out=dcol[0:64, 6:7], in_=AP3(dram["bs"], 0, [[1, 64], [1, 1]]))
        t_c = Y.dtok()
        V.wait(tT0, *t6a)
        V.op("memset", onesf[0:1, :], 1.0)
        for j in range(8):
            guard(T, "PT")
            T.wait(t_w if j == 0 else None)
            T.op("transpose", PT[0:128, 0:4], cwrow[0:4, j * 128:(j + 1) * 128],
                 identf[0:4, 0:4])
            tT = T.tok()
            V.wait(tT)
            V.op("tensor_copy", cwv[:, j * 4:(j + 1) * 4], PT[0:128, 0:4])
            release("PT", V.tok())
        guard(T, 5)
        T.wait(t_c, t_w)
        for j in range(8):
            T.op("matmul", PS[5][:, 0:1], wpv[:, j * 128:(j + 1) * 128],
                 cb8[:, j:j + 1], start=(j == 0), stop=(j == 7),
                 skip_group_check=True)
        t_bp = T.tok()
        V.wait(t_bp)
        V.op("tensor_tensor", bprime[:, 0:1], PS[5][:, 0:1], bprime[:, 1:2], ALU.add)
        release(5, V.tok())
        S.wait(t_c)
        S.op("activation", dcol[0:64, 0:1], dcol[0:64, 1:2], AF.Exp)
        S.op("activation", dcol[0:64, 7:8], dcol[0:64, 2:3], AF.Exp)
        tS = S.tok()
        V.wait(tS)
        V.op("tensor_scalar", dcol[0:64, 0:1], dcol[0:64, 0:1], 1.0, None, ALU.add)
        V.op("tensor_scalar", dcol[0:64, 7:8], dcol[0:64, 7:8], 1.0, None, ALU.add)
        tV = V.tok()
        S.wait(tV)
        S.op("activation", dcol[0:64, 3:4], dcol[0:64, 0:1], AF.Ln)
        S.op("activation", dcol[0:64, 4:5], dcol[0:64, 7:8], AF.Ln)
        tS = S.tok()
        V.wait(tS)
        V.op("tensor_scalar", dcol[0:64, 3:4], dcol[0:64, 3:4], -1.0, None, ALU.mult)
        V.op("tensor_tensor", dcol[0:64, 5:6], dcol[0:64, 4:5], dcol[0:64, 3:4],
             ALU.mult)
        V.op("tensor_scalar", dcol[0:64, 5:6], dcol[0:64, 5:6], 1.0, None, ALU.add)
        V.op("memset", dbc[0:64, :], 1.0)
        V.op("tensor_scalar", dbc[0:64, :], dbc[0:64, :], dcol[0:64, 5:6], None,
             ALU.mult)
        for j in range(8):
            for k in range(CONVK):
                V.op("tensor_scalar", wkb[:, (k * 8 + j) * 128:(k * 8 + j + 1) * 128],
                     wpv[:, j * 128:(j + 1) * 128], cwv[:, j * 4 + k:j * 4 + k + 1],
                     None, ALU.mult)
        t_wk = V.tok()
        for half in range(2):
            t_half = (T.sem, T.cnt)
            for j in range(8):
                S.wait(bg.get("tmpA"), t_half, *t6a)
                S.dma(out=tmpA[:, 0:2048],
                      in_=dram["xn1tok_b"][half * 2048:(half + 1) * 2048,
                                           j * 128:(j + 1) * 128], transpose=True)
                t_tr = S.dtok()
                V.wait(t_tr, t_half)
                V.op("tensor_copy",
                     AP3(xnh, j * 8 * LW + HALO, [[XPITCH, 128], [LW, 8], [1, NPC]]),
                     tmpA[:, 0:2048])
                bg["tmpA"] = V.tok()
                S.wait(bg["tmpA"])
                S.dma(out=tmpA[:, 0:512],
                      in_=dram["halo_d"][half * 512:(half + 1) * 512,
                                         j * 128:(j + 1) * 128], transpose=True)
                t_tr = S.dtok()
                V.wait(t_tr)
                V.op("tensor_copy",
                     AP3(xnh, j * 8 * LW, [[XPITCH, 128], [LW, 8], [1, HALO]]),
                     tmpA[:, 0:512])
                bg["tmpA"] = V.tok()
            t_xnh = V.tok()
            for lh in range(8):
                gl = half * 8 + lh
                guard(T, 0)
                T.wait(t_xnh, t_wk)
                for k in range(CONVK):
                    for j in range(8):
                        T.op("matmul", PS[0][:, 0:SCW],
                             wkb[:, (k * 8 + j) * 128:(k * 8 + j + 1) * 128],
                             AP3(xnh, j * 8 * LW + lh * LW + 13 + k,
                                 [[XPITCH, 128], [1, SCW]]),
                             start=(k == 0 and j == 0), stop=(k == 3 and j == 7),
                             skip_group_check=True)
                t_pj = T.tok()
                guard(T, 1)
                for j in range(8):
                    T.op("matmul", PS[1][0:64, 0:SCW],
                         wsb[:, j * 64:(j + 1) * 64],
                         AP3(xnh, j * 8 * LW + lh * LW + 16, [[XPITCH, 128], [1, SCW]]),
                         start=(j == 0), stop=(j == 7), skip_group_check=True)
                t_u = T.tok()
                S.wait(t_pj, bg.get("BbF"), bg.get("BbFd"), bg.get("zbq"))
                S.op("activation", BbF[0:64, :], PS[0][0:64, 0:SCW], AF.Exp,
                     bias=bprime[0:64, 0:1])
                S.op("activation", BbF[64:128, :], PS[0][64:128, 0:SCW], AF.Tanh,
                     bias=bprime[64:128, 0:1])
                tS = S.tok()
                release(0, tS)
                V.wait(tS)
                V.op("tensor_scalar", BbF[0:64, :], BbF[0:64, :], 1.0, None, ALU.add)
                tV = V.tok()
                S.wait(tV)
                S.op("activation", zb[0:64, :], BbF[0:64, :], AF.Ln)
                tS = S.tok()
                Y.wait(tS, bg.get("Cb"))
                Y.dma(out=dram["csc_d"], in_=BbF[64:128, :])
                t_c1 = Y.dtok()
                Y.wait(t_c1)
                Y.dma(out=Cb[0:64, :], in_=dram["csc_d"])
                t_cb = Y.dtok()
                bg["BbFd"] = t_cb
                V.wait(t_u, bg.get("ub"))
                V.op("tensor_scalar", ub[0:64, :], PS[1][0:64, 0:SCW],
                     dcol[0:64, 6:7], None, ALU.add)
                release(1, V.tok())
                V.wait(tS)
                V.op("scalar_tensor_tensor", zb[0:64, :], zb[0:64, :],
                     dcol[0:64, 4:5], ub[0:64, :], ALU.mult, ALU.mult)
                bg["BbF"] = V.tok()
                bg["ub"] = bg["BbF"]
                bg["zbq"] = bg["BbF"]
                V.op("tensor_tensor", zb[0:64, :], zb[0:64, :], maskS[0:64, :],
                     ALU.mult)
                V.op("tensor_tensor_scan", yscan[0:64, :], dbc[0:64, :], zb[0:64, :],
                     0.0, ALU.mult, ALU.add)
                V.wait(t_cb)
                V.op("tensor_tensor", yscan[0:64, :], yscan[0:64, :], Cb[0:64, :],
                     ALU.mult)
                t_y = V.tok()
                bg["Cb"] = t_y
                for pb in range(2):
                    for dh in range(2):
                        guard(T, 2 + dh)
                        T.wait(t_y if (pb == 0 and dh == 0) else None)
                        T.op("matmul", PS[2 + dh][:, 0:512],
                             yscan[0:64, 48 + pb * 128:48 + pb * 128 + 128],
                             wlo[0:64, dh * 512:(dh + 1) * 512],
                             start=True, stop=False, skip_group_check=True)
                        T.op("matmul", PS[2 + dh][:, 0:512],
                             onesf[0:1, 0:128], blorow[0:1, dh * 512:(dh + 1) * 512],
                             start=False, stop=True, skip_group_check=True)
                        tmm = T.tok()
                        V.wait(tmm, bg.get("s1t") if dh == 0 else None)
                        V.op("tensor_copy", s1t[:, dh * 512:(dh + 1) * 512],
                             PS[2 + dh][:, 0:512])
                        release(2 + dh, V.tok())
                    tV = V.tok()
                    Y.wait(tV)
                    Y.dma(out=dram["s1tok_d"][gl * 256 + pb * 128:
                                              gl * 256 + pb * 128 + 128, :],
                          in_=s1t[:, :])
                    bg["s1t"] = Y.dtok()
        return all_dtoks(Y)

    t_s1 = emit_p6b_attn()
    if STOP_AFTER == "p6attn":
        env = dict(locals())
        if DBG_HOOK[0] is not None:
            DBG_HOOK[0](p, env)
        Y.wait(*all_dtoks(Y), *all_dtoks(S), *all_dtoks(G))
        Y.op("nop")
        p.play()
        esA.close()
        es.close()
        return nc, dram

    esA.close()
    esF = ExitStack()

    def sbF(name, shape, dtype):
        return esF.enter_context(nc.sbuf_tensor(name, shape, dtype))

    wqt = sbF("wqt", [128, HID], F32)
    wqb16 = sbF("wqb16", [128, HID], BF16)
    q6b = sbF("q6b", [128, 512], F32)
    xqT = sbF("xqT", [128, D], BF16)
    hT6 = sbF("hT6", [128, HID], F32)
    hqb = sbF("hqb", [128, HID], BF16)
    hqT = sbF("hqT", [128, HID], BF16)
    g1 = sbF("g1", [128, D], F32)
    x1f = sbF("x1f", [128, D], F32)
    s1f = sbF("s1f", [128, D], F32)

    def ar6(j, off, w):
        if j < 6:
            return wA[:, j * 4096 + off:j * 4096 + off + w]
        return wB[:, (j - 6) * 4096 + off:(j - 6) * 4096 + off + w]

    def ar1(fj, off, w):
        if fj < 24:
            return wA[:, fj * 1024 + off:fj * 1024 + off + w]
        return wB[:, (fj - 24) * 1024 + off:(fj - 24) * 1024 + off + w]

    def emit_p6_ffn():
        tT0 = (T.sem, T.cnt)
        tV0 = (V.sem, V.cnt)
        tS0 = (S.sem, S.cnt)
        for ch in range(32):
            Y.wait(bg.get("wqt") or tV0, tT0, *t_s1)
            Y.dma(out=wqt[:, 0:D], in_=dram["w_fc"][ch * 128:(ch + 1) * 128, :])
            t_ld = Y.dtok()
            V.wait(t_ld, tT0, tS0, bg.get("wq_st"))
            t_q = quant_apply(wqt, q6b, wqb16, D, HID, 512)
            bg["wqt"] = t_q
            Y.wait(t_q)
            Y.dma(out=dram["wfcq_d"][ch * 128:(ch + 1) * 128, :], in_=wqb16[:, 0:D])
            bg["wq_st"] = Y.dtok()
        for ch in range(8):
            Y.wait(bg.get("wqt"))
            Y.dma(out=wqt[:, :], in_=dram["w_pr"][ch * 128:(ch + 1) * 128, :])
            t_ld = Y.dtok()
            V.wait(t_ld, bg.get("wq_st"))
            t_q = quant_apply(wqt, q6b, wqb16, HID, HID, 512)
            bg["wqt"] = t_q
            Y.wait(t_q)
            Y.dma(out=dram["wprq_d"][ch * 128:(ch + 1) * 128, :], in_=wqb16[:, :])
            bg["wq_st"] = Y.dtok()
        G.wait(tT0)
        G.dma(out=bias_b[0:1, 0:HID], in_=dram["b_fc"])
        t_bf = G.dtok()
        t_wf = all_dtoks(Y)
        S.wait(tT0, *t_wf)
        for j in range(8):
            S.dma(out=ar6(j, 0, 4096),
                  in_=dram["wfcq_d"][:, j * 128:(j + 1) * 128], transpose=True)
        t_ar = S.dtok()
        for gr in range(NG):
            S.wait(bg.get("xqT"), *t6a)
            for j in range(8):
                S.dma(out=xqT[:, j * 128:(j + 1) * 128],
                      in_=dram["xq_d"][gr * 128:(gr + 1) * 128,
                                       j * 128:(j + 1) * 128], transpose=True)
            t_x = S.dtok()
            for f8 in range(8):
                bank = PS[f8 % 2]
                guard(T, f8 % 2)
                T.wait(t_x if f8 == 0 else None, t_ar if f8 == 0 else None,
                       t_bf if f8 == 0 else None)
                for j in range(8):
                    T.op("matmul", bank[:, 0:512], xqT[:, j * 128:(j + 1) * 128],
                         ar6(j, f8 * 512, 512),
                         start=(j == 0), stop=False, skip_group_check=True)
                T.op("matmul", bank[:, 0:512], onesb[0:1, 0:128],
                     bias_b[0:1, f8 * 512:(f8 + 1) * 512],
                     start=False, stop=True, skip_group_check=True)
                tmm = T.tok()
                S.wait(tmm, bg.get("hT6") if f8 < 2 else None)
                S.op("activation", hT6[:, f8 * 512:(f8 + 1) * 512], bank[:, 0:512],
                     AF.Gelu)
                release(f8 % 2, S.tok())
            tS = S.tok()
            bg["xqT"] = (T.sem, T.cnt)
            V.wait(tS, bg.get("hqb"))
            t_q = quant_apply(hT6, q6b, hqb, HID, HID, 512)
            bg["hT6"] = t_q
            Y.wait(t_q, bg.get("hqb"))
            Y.dma(out=dram["hq_d"][gr * 128:(gr + 1) * 128, :], in_=hqb[:, :])
            bg["hqb"] = Y.dtok()
        t_hq = all_dtoks(Y)
        tTf = (T.sem, T.cnt)
        G.wait(tTf)
        G.dma(out=bias_b[0:1, 0:D], in_=dram["b_pr"])
        t_bp2 = G.dtok()
        S.wait(tTf, *t_hq)
        for fj in range(32):
            S.dma(out=ar1(fj, 0, 1024),
                  in_=dram["wprq_d"][:, fj * 128:(fj + 1) * 128], transpose=True)
        t_ar2 = S.dtok()
        for gr in range(NG):
            S.wait(bg.get("hqT"), *t_hq)
            for fj in range(32):
                S.dma(out=hqT[:, fj * 128:(fj + 1) * 128],
                      in_=dram["hq_d"][gr * 128:(gr + 1) * 128,
                                       fj * 128:(fj + 1) * 128], transpose=True)
            t_h = S.dtok()
            Y.wait(bg.get("x1f"))
            Y.dma(out=x1f[:, :], in_=dram["x1tok_d"][gr * 128:(gr + 1) * 128, :])
            Y.dma(out=s1f[:, :], in_=dram["s1tok_d"][gr * 128:(gr + 1) * 128, :])
            t_xs = Y.dtok()
            for dh in range(2):
                bank = PS[2 + dh]
                guard(T, 2 + dh)
                T.wait(t_h if dh == 0 else None, t_ar2 if dh == 0 else None,
                       t_bp2 if dh == 0 else None)
                for fj in range(32):
                    T.op("matmul", bank[:, 0:512], hqT[:, fj * 128:(fj + 1) * 128],
                         ar1(fj, dh * 512, 512),
                         start=(fj == 0), stop=False, skip_group_check=True)
                T.op("matmul", bank[:, 0:512], onesb[0:1, 0:128],
                     bias_b[0:1, dh * 512:(dh + 1) * 512],
                     start=False, stop=True, skip_group_check=True)
                tmm = T.tok()
                S.wait(tmm, bg.get("g1") if dh == 0 else None,
                       bg.get("g1out") if dh == 0 else None)
                S.op("activation", g1[:, dh * 512:(dh + 1) * 512], bank[:, 0:512],
                     AF.Silu)
                release(2 + dh, S.tok())
            tS = S.tok()
            bg["hqT"] = (T.sem, T.cnt)
            V.wait(tS, t_xs)
            V.op("tensor_tensor", g1[:, :], g1[:, :], s1f[:, :], ALU.mult)
            V.op("tensor_tensor", g1[:, :], g1[:, :], x1f[:, :], ALU.add)
            t_o = V.tok()
            bg["g1"] = t_o
            bg["x1f"] = t_o
            Y.wait(t_o)
            Y.dma(out=out_d[gr * 128:(gr + 1) * 128, :], in_=g1[:, :])
            bg["g1out"] = Y.dtok()
        return all_dtoks(Y)

    t_out = emit_p6_ffn()

    env = dict(locals())
    if DBG_HOOK[0] is not None:
        DBG_HOOK[0](p, env)

    Y.wait(*all_dtoks(Y), *all_dtoks(S), *all_dtoks(G))
    Y.op("nop")
    p.play()
    esF.close()
    es.close()
    return nc, dram



def _host_inputs(inputs, core):
    x = np.ascontiguousarray(inputs["L0_attn_x"])
    n0 = core * NPC
    xs = np.ascontiguousarray(x[:, n0:n0 + NPC, :]).reshape(TPC, D)
    m = {
        "x0tok": xs,
        "x0T": np.ascontiguousarray(xs.T),
        "w_qkvT": np.ascontiguousarray(inputs["L0_ffn_in_proj_w"].T),
        "b_qkv": inputs["L0_ffn_in_proj_b"].reshape(1, 3 * D),
        "w_oT": np.ascontiguousarray(inputs["L0_ffn_out_proj_w"].T),
        "b_o": inputs["L0_ffn_out_proj_b"].reshape(1, D),
        "ln1_g": inputs["L0_ffn_ln1_g"].reshape(1, D),
        "ln1_b": inputs["L0_ffn_ln1_b"].reshape(1, D),
        "ln2_g": inputs["L0_ffn_ln2_g"].reshape(1, D),
        "ln2_b": inputs["L0_ffn_ln2_b"].reshape(1, D),
        "wg": inputs["L0_ffn_wg"],
        "w_q": inputs["L0_attn_Wq"], "b_q": inputs["L0_attn_bq"].reshape(1, D),
        "w_k": inputs["L0_attn_Wk"], "b_k": inputs["L0_attn_bk"].reshape(1, D),
        "w_v": inputs["L0_attn_Wv"], "b_v": inputs["L0_attn_bv"].reshape(1, D),
        "w_ao": inputs["L0_attn_Wo"], "b_ao": inputs["L0_attn_bo"].reshape(1, D),
        "w1e": inputs["L0_ffn_w1"][core],
        "b1e": inputs["L0_ffn_b1"][core].reshape(1, FF),
        "w2e": inputs["L0_ffn_w2"][core],
        "b2e": inputs["L0_ffn_b2"][core].reshape(1, D),
        "conv_w": inputs["L1_attn_conv_w"].reshape(CONVK, D),
        "conv_b": inputs["L1_attn_conv_b"].reshape(1, D),
        "wp": inputs["L1_attn_Wp"], "bp": inputs["L1_attn_bp"].reshape(1, 2 * NST),
        "ws": inputs["L1_attn_Ws"], "bs": inputs["L1_attn_bs"].reshape(1, NST),
        "a_log": inputs["L1_attn_A_log"].reshape(1, NST),
        "dt_log": inputs["L1_attn_dt_log"].reshape(1, NST),
        "w_loT": inputs["L1_attn_Wo"], "b_lo": inputs["L1_attn_bo"].reshape(1, D),
        "w_fc": inputs["L1_ffn_w_fc"], "b_fc": inputs["L1_ffn_b_fc"].reshape(1, HID),
        "w_prT": np.ascontiguousarray(inputs["L1_ffn_w_proj"].T),
        "b_pr": inputs["L1_ffn_b_proj"].reshape(1, D),
        "w_pr": inputs["L1_ffn_w_proj"],
    }
    mp = np.zeros((128, 512), np.float32)
    for cp in range(NCORES):
        for lp in range(B):
            for sub in range(4):
                for pp_ in range(128):
                    l = 4 * sub + pp_ // 32
                    if lp < l or (lp == l and cp < core):
                        mp[cp * B + lp, sub * 128 + pp_] = 1.0
    m["maskP"] = mp
    mm = np.zeros((128, 512), np.float32)
    for row in range(128):
        for col in range(512):
            a = col // 32
            par = (col % 32) // 16
            if row // 32 == a % 4 and (row % 32) // 16 == par:
                mm[row, col] = 1.0
    m["mham_mask"] = mm
    TAILR = B * HALO + 16
    hi = np.zeros((B * HALO, 1), np.int32)
    if core == 0:
        k = np.arange(B * HALO) % 128
        hi[:, 0] = (k // 16) * TAILR + B * HALO + (k % 16)
    else:
        hi[:, 0] = (core - 1) * TAILR + np.arange(B * HALO)
    m["halo_idx"] = hi
    sm = np.ones((1, SCW), np.float32)
    if core == 0:
        sm[0, 0:48] = 0.0
    m["scan_mask"] = sm
    return m


_CACHE = {}


def _get_program():
    if "nc" not in _CACHE:
        nc, dram = build_program()
        _CACHE["nc"] = nc
        _CACHE["dram"] = dram
    return _CACHE["nc"], _CACHE["dram"]


def kernel(**inputs):
    nc, dram = _get_program()
    in_maps = [_host_inputs(inputs, c) for c in range(NCORES)]
    res = run_bass_kernel_spmd(nc, in_maps, core_ids=list(range(NCORES)))
    out = np.empty((B, TG, D), np.float32)
    for c in range(NCORES):
        out[:, c * NPC:(c + 1) * NPC, :] = res.results[c]["out"].reshape(B, NPC, D)
    return out
